# version 02 (final)
# baseline (speedup 1.0000x reference)
import numpy as np
import ml_dtypes
from contextlib import ExitStack

import concourse.bass as bass
import concourse.mybir as mybir
from concourse.bass_utils import run_bass_kernel_spmd
from concourse.alu_op_type import AluOpType as OP

F32 = mybir.dt.float32
BF16 = mybir.dt.bfloat16
I32 = mybir.dt.int32
AF = mybir.ActivationFunctionType
AX = mybir.AxisListType

QSCALE = float(128.0 ** -0.5)
EPS = 1e-5
NEG = -1e9


class Prog:

    ENGINES = ("t", "v", "s", "g", "y")

    def __init__(self, sems):
        self.sems = sems
        self.cnt = {k: 0 for k in sems}
        self.ops = {e: [] for e in self.ENGINES}

    def op(self, eng, fn, waits=(), inc=None, n=1):
        w = {}
        for m in waits:
            if m is None:
                continue
            s, v = m
            if w.get(s, 0) < v:
                w[s] = v
        mark = None
        if inc is not None:
            self.cnt[inc] += n
            mark = (inc, self.cnt[inc])
        self.ops[eng].append((w, fn, inc, n))
        return mark

    def emit_all(self, key, engine):
        watermark = {}
        for w, fn, inc, n in self.ops[key]:
            for s in sorted(w):
                v = w[s]
                if watermark.get(s, 0) < v:
                    engine.wait_ge(self.sems[s], v)
                    watermark[s] = v
            instr = fn(engine)
            if inc is not None:
                instr.then_inc(self.sems[inc], n)


def build(debug=0, stop=0):
    nc = bass.Bass("TRN2", target_bir_lowering=False)

    d_xto = nc.dram_tensor("xto", [2048, 1024], F32, kind="ExternalInput").ap()
    d_xhp = nc.dram_tensor("xhp", [2048, 144], F32, kind="ExternalInput").ap()
    d_masks = nc.dram_tensor("maskr", [128, 816], BF16, kind="ExternalInput").ap()
    d_ident = nc.dram_tensor("identw", [128, 128], BF16, kind="ExternalInput").ap()
    d_qs = nc.dram_tensor("qsv", [128, 160], F32, kind="ExternalInput").ap()
    d_wmod = nc.dram_tensor("wmodr", [96, 128, 16, 128], F32, kind="ExternalInput").ap()
    d_wkvc = nc.dram_tensor("wkvc", [16, 128, 4, 128], F32, kind="ExternalInput").ap()
    d_wkvl = nc.dram_tensor("wkvl", [4, 128, 16, 128], F32, kind="ExternalInput").ap()
    d_wq = nc.dram_tensor("wqr", [16, 128, 16, 128], F32, kind="ExternalInput").ap()
    d_wkt = nc.dram_tensor("wktr", [128, 64, 128], F32, kind="ExternalInput").ap()
    d_wv = nc.dram_tensor("wvr", [128, 4, 2048], F32, kind="ExternalInput").ap()
    d_wo = nc.dram_tensor("wor", [16, 128, 16, 128], F32, kind="ExternalInput").ap()
    d_gw = nc.dram_tensor("gwr", [64, 128, 16, 128], I32, kind="ExternalInput").ap()
    d_uw = nc.dram_tensor("uwr", [64, 128, 16, 128], I32, kind="ExternalInput").ap()
    d_dw = nc.dram_tensor("dwr", [16, 4, 128, 16, 128], I32, kind="ExternalInput").ap()
    d_out = nc.dram_tensor("out", [2048, 1024], F32, kind="ExternalOutput").ap()
    dbg = {}
    if debug:
        for name, width, dt in (("h1", 16384, BF16), ("kvb", 4672, BF16), ("q", 16384, BF16),
                                ("oa", 16384, BF16), ("x1", 16384, F32), ("h2", 16384, BF16)):
            dbg[name] = nc.dram_tensor("dbg_" + name, [128, width], dt, kind="ExternalOutput").ap()

    with ExitStack() as stk:
        e = stk.enter_context
        x_sb = e(nc.sbuf_tensor("x_sb", [128, 16384], F32))
        xhb = e(nc.sbuf_tensor("xhb", [128, 288], BF16))
        masks = e(nc.sbuf_tensor("masks", [128, 816], BF16))
        ident = e(nc.sbuf_tensor("ident", [128, 128], BF16))
        prm = e(nc.sbuf_tensor("prm", [128, 208], F32))
        qs = e(nc.sbuf_tensor("qs", [128, 160], F32))
        siluc = e(nc.sbuf_tensor("siluc", [128, 16], BF16))
        siluc_f = e(nc.sbuf_tensor("siluc_f", [128, 16], F32))
        scomb = e(nc.sbuf_tensor("scomb", [128, 32], BF16))
        ones1f = e(nc.sbuf_tensor("ones1f", [1, 128], F32))
        ones_k = e(nc.sbuf_tensor("ones_k", [128, 1], BF16))
        abig = e(nc.sbuf_tensor("abig", [128, 16384], BF16))
        q2 = e(nc.sbuf_tensor("q2", [128, 16384], BF16))
        kvb = e(nc.sbuf_tensor("kvb", [128, 4672], BF16))
        kva = e(nc.sbuf_tensor("kva", [128, 8192], BF16))
        wkt = e(nc.sbuf_tensor("wkt", [128, 8192], BF16))
        wst = e(nc.sbuf_tensor("wst", [128, 4096], BF16))
        wv = e(nc.sbuf_tensor("wv", [128, 8192], BF16))
        smx = e(nc.sbuf_tensor("smx", [128, 1312], BF16))
        scs = e(nc.sbuf_tensor("scs", [128, 544], F32))
        rows = e(nc.sbuf_tensor("rows", [1, 1024], F32))
        aux = e(nc.sbuf_tensor("aux", [128, 8], F32))

        pAB = e(nc.psum_tensor("pAB", [128, 1024], F32))
        pCD = e(nc.psum_tensor("pCD", [128, 1024], F32))
        pEF = e(nc.psum_tensor("pEF", [128, 1024], F32))
        pG = e(nc.psum_tensor("pG", [128, 512], F32))
        pT = e(nc.psum_tensor("pT", [128, 1024], BF16))

        sems = {n: e(nc.semaphore(n)) for n in ("dW", "dX", "tS", "vS", "sS")}
        p = Prog(sems)

        def DW(src, dst, waits=(), **kw):
            return p.op("g", lambda g, s=src, d=dst, kw=kw: g.dma_start(d, s, **kw), waits, inc="dW", n=16)

        def DX(src, dst, waits=()):
            return p.op("y", lambda y, s=src, d=dst: y.dma_start(d, s), waits, inc="dX", n=16)

        def T(fn, waits=(), inc=False):
            return p.op("t", fn, waits, inc="tS" if inc else None)

        def V(fn, waits=(), inc=True):
            return p.op("v", fn, waits, inc="vS" if inc else None)

        def S(fn, waits=(), inc=True):
            return p.op("s", fn, waits, inc="sS" if inc else None)

        def emit_block():
            with nc.Block() as block:
                @block.tensor
                def _(t):
                    p.emit_all("t", t)

                @block.vector
                def _(v):
                    p.emit_all("v", v)

                @block.scalar
                def _(s):
                    p.emit_all("s", s)

                @block.gpsimd
                def _(g):
                    p.emit_all("g", g)

                @block.sync
                def _(y):
                    p.emit_all("y", y)
            return nc

        def stop_now(marks):
            p.op("y", lambda y: y.dma_start(d_out.rearrange("(c p) s -> p c s", p=128),
                                            x_sb[:].rearrange("p (c s) -> p c s", c=16)),
                 list(marks), inc="dX", n=16)
            p.op("y", lambda y: y.nop(), [("dX", p.cnt["dX"])])
            return emit_block()

        m_masks = DW(d_masks, masks[:])
        m_ident = DW(d_ident, ident[:])
        m_qs = DW(d_qs, qs[:])
        m_x = DW(d_xto.rearrange("(c p) s -> p c s", p=128), x_sb[:])
        m_ones = V(lambda v: v.memset(ones1f[:], 1.0))
        m_onesk = V(lambda v: v.memset(ones_k[:], 1.0))
        m_silf = S(lambda s: s.activation(siluc_f[:], qs[:, 144:160], AF.Silu), [m_qs])
        m_siluc = S(lambda s: s.activation(siluc[:], siluc_f[:], AF.Identity), [m_silf])
        m_shi = S(lambda s: s.activation(scomb[:, 0:32:2], siluc_f[:], AF.Identity), [m_silf])
        m_scomb = V(lambda v: v.scalar_tensor_tensor(scomb[:, 1:32:2], scomb[:, 0:32:2], -1.0,
                                                     siluc_f[:], OP.mult, OP.add), [m_shi])
        for l in range(4):
            m_wkt = DW(d_wkt[:, 16 * l:16 * (l + 1), :], wkt[:, 2048 * l:2048 * (l + 1)])
            m_wv = DW(d_wv[:, l, :], wv[:, 2048 * l:2048 * (l + 1)])

        wst_free = [None, None]

        def mod_chunks(j0, j1, pcol_fn, extra=()):
            mm = None
            for j in range(j0, j1):
                slot = j % 2
                slab = wst[:, slot * 2048:(slot + 1) * 2048]
                mdma = DW(d_wmod[j], slab, [wst_free[slot]])
                pc = pcol_fn(j)
                for d in range(16):
                    mm = T(lambda t, pc=pc, sl=slab, d=d: t.matmul(
                        pc, sl[:, d * 128:(d + 1) * 128], siluc[:, d:d + 1],
                        start=(d == 0), stop=(d == 15)),
                        ([mdma, m_siluc] + list(extra)) if d == 0 else [], inc=(d == 15))
                wst_free[slot] = mm
            return mm

        mk0 = mod_chunks(0, 32, lambda j: pAB[:, j:j + 1])
        m_prm_a = V(lambda v: v.tensor_copy(prm[:, 0:32], pAB[:, 0:32]), [mk0])
        m_ca1 = V(lambda v: v.tensor_scalar(prm[:, 96:112], prm[:, 16:32], 1.0, None, OP.add), [m_prm_a])

        def ln_stats(w, sum_mark):
            mm_ = V(lambda v, w=w: v.tensor_scalar(rows[0:1, 0:w], pCD[0:1, 0:w], 1.0 / 2048.0, None, OP.mult),
                    [sum_mark])
            mr = V(lambda v, w=w: v.tensor_tensor(rows[0:1, 512:512 + w], rows[0:1, 0:w], rows[0:1, 0:w], OP.mult),
                   [mm_])
            mr = V(lambda v, w=w: v.scalar_tensor_tensor(rows[0:1, 512:512 + w], pCD[0:1, 512:512 + w],
                                                         1.0 / 2048.0, rows[0:1, 512:512 + w],
                                                         OP.mult, OP.subtract), [mr])
            mr = V(lambda v, w=w: v.tensor_scalar(rows[0:1, 512:512 + w], rows[0:1, 512:512 + w], EPS, None, OP.add),
                   [mr])
            mr = S(lambda s, w=w: s.activation(rows[0:1, 512:512 + w], rows[0:1, 512:512 + w], AF.Ln), [mr])
            mr = S(lambda s, w=w: s.activation(rows[0:1, 512:512 + w], rows[0:1, 512:512 + w], AF.Exp, scale=-0.5),
                   [mr])
            bm = T(lambda t, w=w: t.matmul(pEF[:, 0:w], ones1f[:], rows[0:1, 0:w], start=True, stop=True),
                   [mm_, m_ones], inc=True)
            br = T(lambda t, w=w: t.matmul(pEF[:, 512:512 + w], ones1f[:], rows[0:1, 512:512 + w],
                                           start=True, stop=True), [mr], inc=True)
            return bm, br

        def emit_ln1a():
            w = 144
            sum_mark = None
            xhb_prev = [None, None]
            for c in range(16):
                par = c % 2
                xb = xhb[:, par * 144:(par + 1) * 144]
                ma = DW(d_xhp[128 * c:128 * (c + 1), :], xb, [xhb_prev[par]])
                xsq = kva[:, 0:144]
                msq = V(lambda v, xb=xb, xsq=xsq: v.tensor_tensor(xsq, xb, xb, OP.mult), [ma, sum_mark])
                T(lambda t, xb=xb, c=c: t.matmul(pCD[0:1, 0:144], ones_k[:], xb, start=(c == 0), stop=(c == 15)),
                  [ma, m_onesk] if c == 0 else [ma])
                sum_mark = T(lambda t, xsq=xsq, c=c: t.matmul(pCD[0:1, 512:656], ones_k[:], xsq,
                                                              start=(c == 0), stop=(c == 15)), [msq], inc=True)
                xhb_prev[par] = sum_mark
            bm, br = ln_stats(144, sum_mark)
            last_kv = None
            app_prev = [None, None]
            slab_prev = [None, None]
            hh = smx[:, 1024:1168]
            for c in range(16):
                par = c % 2
                xb = xhb[:, par * 144:(par + 1) * 144]
                ma = DW(d_xhp[128 * c:128 * (c + 1), :], xb, [sum_mark, app_prev[par]])
                slab = kva[:, 1024 + par * 512:1024 + (par + 1) * 512]
                ms = DW(d_wkvc[c], slab, [slab_prev[par]])
                tb = kva[:, 512:656]
                a1 = V(lambda v, xb=xb, tb=tb: v.tensor_tensor(tb, xb, pEF[:, 0:144], OP.subtract),
                       [bm, ma, last_kv])
                a2 = V(lambda v, tb=tb: v.tensor_tensor(tb, tb, pEF[:, 512:656], OP.mult), [br, a1])
                happ = S(lambda s, tb=tb, c=c: s.activation(hh, tb, AF.Identity,
                                                            scale=prm[:, 96 + c:97 + c], bias=prm[:, c:c + 1]),
                         [a2, m_ca1, last_kv])
                app_prev[par] = happ
                for l in range(4):
                    pk = (pCD[:, 0:144], pCD[:, 512:656], pAB[:, 0:144], pAB[:, 512:656])[l]
                    last_kv = T(lambda t, slab=slab, l=l, pk=pk, c=c: t.matmul(
                        pk, slab[:, l * 128:(l + 1) * 128], hh,
                        start=(c == 0), stop=(c == 15)),
                        ([happ, ms] + ([sum_mark, m_prm_a] if c == 0 else [])), inc=(l == 3))
                slab_prev[par] = last_kv
            ev = None
            for l in range(4):
                pk = (pCD[:, 0:144], pCD[:, 512:656], pAB[:, 0:144], pAB[:, 512:656])[l]
                ev = S(lambda s, l=l, pk=pk: s.activation(kvb[:, 1168 * l:1168 * l + 144],
                                                          pk, AF.Identity), [last_kv])
            return ev, a2

        kv_halo_done, ln1a_lastread = emit_ln1a()

        def ln_own(src, dst, ca1_off, sa_off, prm_mark, x_ready, bank_free):
            last_apps = []
            for gi in range(2):
                goff = gi * 512
                sum_mark = None
                cast_prev = [None, None]
                for c in range(16):
                    par = c % 2
                    xb = smx[:, par * 512:(par + 1) * 512]
                    mcast = S(lambda s, xb=xb, c=c, goff=goff, src=src: s.activation(
                        xb, src[:, 1024 * c + goff:1024 * c + goff + 512], AF.Identity),
                        [x_ready, cast_prev[par]])
                    xsq = kva[:, 0:512]
                    msq = V(lambda v, xb=xb, xsq=xsq: v.tensor_tensor(xsq, xb, xb, OP.mult), [mcast, sum_mark])
                    T(lambda t, xb=xb, c=c: t.matmul(pCD[0:1, 0:512], ones_k[:], xb,
                                                     start=(c == 0), stop=(c == 15)),
                      ([mcast, m_onesk] + list(bank_free)) if c == 0 else [mcast])
                    sum_mark = T(lambda t, xsq=xsq, c=c: t.matmul(pCD[0:1, 512:1024], ones_k[:], xsq,
                                                                  start=(c == 0), stop=(c == 15)),
                                 [msq], inc=True)
                    cast_prev[par] = sum_mark
                bm, br = ln_stats(512, sum_mark)
                last_app = None
                for c in range(16):
                    tb = kva[:, 512:1024]
                    a1 = V(lambda v, c=c, goff=goff, tb=tb, src=src: v.tensor_tensor(
                        tb, src[:, 1024 * c + goff:1024 * c + goff + 512], pEF[:, 0:512], OP.subtract),
                        [bm, last_app])
                    a2 = V(lambda v, tb=tb: v.tensor_tensor(tb, tb, pEF[:, 512:1024], OP.mult), [br, a1])
                    last_app = S(lambda s, tb=tb, c=c, goff=goff, dst=dst: s.activation(
                        dst[:, 1024 * c + goff:1024 * c + goff + 512], tb, AF.Identity,
                        scale=prm[:, ca1_off + c:ca1_off + c + 1], bias=prm[:, sa_off + c:sa_off + c + 1]),
                        [a2, prm_mark])
                    last_apps.append(last_app)
                bank_free = [last_app]
            return last_apps[-1]

        h1_mark = ln_own(x_sb, abig, 96, 0, m_ca1, m_x, [kv_halo_done])

        modp = {"mn": None}

        def stream_slab(src_ap, slot, waits=()):
            slab = wst[:, slot * 2048:(slot + 1) * 2048]
            m = DW(src_ap, slab, [wst_free[slot], modp["mn"] if slot == 0 else None] + list(waits))
            return slab, m

        kv_evs = []
        kv_mm_prev = m_prm_a
        for l in range(4):
            slot = l % 2
            slab, mdma = stream_slab(d_wkvl[l], slot)
            last = None
            for gi in range(2):
                for c in range(16):
                    last = T(lambda t, slab=slab, c=c, gi=gi: t.matmul(
                        pAB[:, gi * 512:(gi + 1) * 512], slab[:, c * 128:(c + 1) * 128],
                        abig[:, 1024 * c + gi * 512:1024 * c + gi * 512 + 512],
                        start=(c == 0), stop=(c == 15)),
                        [mdma, h1_mark, kv_mm_prev] if c == 0 else [], inc=(c == 15))
            ev = None
            for gi in range(2):
                off = 144 + gi * 512
                ev = S(lambda s, l=l, off=off, gi=gi: s.activation(
                    kvb[:, 1168 * l + off:1168 * l + off + 512], pAB[:, gi * 512:(gi + 1) * 512], AF.Identity),
                    [last])
            kv_mm_prev = ev
            wst_free[slot] = last
            kv_evs.append(ev)
        kv_done = kv_evs[-1]

        q_psum_free = {0: kv_done, 1: None}
        q_evs = []
        for o in range(16):
            slot = o % 2
            slab, mdma = stream_slab(d_wq[o], slot)
            ps = pAB if o % 2 == 0 else pCD
            last = None
            for gi in range(2):
                for d in range(16):
                    last = T(lambda t, slab=slab, d=d, gi=gi, ps=ps: t.matmul(
                        ps[:, gi * 512:(gi + 1) * 512], slab[:, d * 128:(d + 1) * 128],
                        abig[:, 1024 * d + gi * 512:1024 * d + gi * 512 + 512],
                        start=(d == 0), stop=(d == 15)),
                        [mdma, h1_mark, q_psum_free[o % 2]] if (d == 0) else [], inc=(d == 15))
            ev = None
            for gi in range(2):
                ev = S(lambda s, o=o, gi=gi, ps=ps: s.activation(
                    q2[:, 1024 * o + gi * 512:1024 * o + gi * 512 + 512],
                    ps[:, gi * 512:(gi + 1) * 512], AF.Identity, scale=QSCALE), [last])
            q_psum_free[o % 2] = ev
            wst_free[slot] = last
            q_evs.append(ev)
        q_mark = q_evs[-1]

        if debug >= 1:
            DX(abig[:], dbg["h1"], [h1_mark, kv_done, q_mark])
            DX(kvb[:], dbg["kvb"], [kv_done, kv_halo_done])
        if debug >= 2:
            DX(q2[:], dbg["q"], [q_mark])

        if stop == 1:
            return stop_now([h1_mark, kv_done, q_mark, kv_halo_done])
        vper = kva[:, 4608:6656]
        vslot = [kva[:, 0:2048], kva[:, 2048:4096]]

        pef_free = {0: None, 1: None}

        def produce_v(j, dst, region_free):
            cols = (0, 16) if j == 0 else (16 + 128 * (j - 1), 16 + 128 * j)
            wid = cols[1] - cols[0]
            ev = None
            for gi in range(4):
                par = gi % 2
                last = None
                for l in range(4):
                    last = T(lambda t, l=l, gi=gi, par=par, cols=cols, wid=wid: t.matmul(
                        pEF[0:wid, par * 512:par * 512 + 512],
                        kvb[:, 1168 * l + cols[0]:1168 * l + cols[1]],
                        wv[:, 2048 * l + gi * 512:2048 * l + gi * 512 + 512],
                        start=(l == 0), stop=(l == 3)),
                        ([kv_done, kv_halo_done, m_wv, pef_free[par]] + list(region_free)) if l == 0 else [],
                        inc=(l == 3))
                ev = S(lambda s, gi=gi, par=par, dst=dst, wid=wid: s.activation(
                    dst[0:wid, gi * 512:gi * 512 + 512], pEF[0:wid, par * 512:par * 512 + 512],
                    AF.Identity), [last])
                pef_free[par] = ev
            return ev

        if stop == 23:
            return stop_now([h1_mark, kv_done, q_mark, kv_halo_done, m_wkt, m_wv])
        vper_mark = produce_v(0, vper, [h1_mark])
        vslot_mark = [None, None]
        vslot_readers = [[], []]
        vslot_mark[1] = produce_v(1, vslot[1], [])

        qk_region = kva[:, 4096:4608]
        qk_free = None

        def modp_chunk(j, col0, extra=()):
            slotA = wst[:, 0:2048]
            slotB = wst[:, 2048:4096]
            mc = DW(d_wmod[j], slotA, [wst_free[0], modp["mn"]])
            mn = V(lambda v: v.tensor_scalar(slotB, slotA, -1.0, None, OP.mult), [mc, wst_free[1]])
            modp["mn"] = mn
            ma = DW(d_wmod[j], slotB, [mn], accum_op=OP.add)
            mmA = None
            for d in range(16):
                mmA = T(lambda t, d=d, col0=col0: t.matmul(
                    pG[:, col0:col0 + 2], wst[:, d * 128:(d + 1) * 128], scomb[:, 2 * d:2 * d + 2],
                    start=(d == 0), stop=False),
                    ([mc, m_scomb] + list(extra)) if d == 0 else [], inc=(d == 15))
            mmB = None
            for d in range(16):
                mmB = T(lambda t, d=d, col0=col0: t.matmul(
                    pG[:, col0:col0 + 2], wst[:, 2048 + d * 128:2048 + (d + 1) * 128], scomb[:, 2 * d:2 * d + 2],
                    start=False, stop=(d == 15)),
                    [ma] if d == 0 else [], inc=(d == 15))
            wst_free[0] = mmA
            wst_free[1] = mmB
            return mmB

        def modp_combine(raw_off, nchunks, last_mm):
            t1 = V(lambda v: v.tensor_scalar(prm[:, 160:160 + nchunks], pG[:, 0:2 * nchunks:2],
                                             1.0, None, OP.mult), [last_mm])
            return V(lambda v: v.tensor_tensor(prm[:, raw_off:raw_off + nchunks],
                                               prm[:, 160:160 + nchunks], pG[:, 1:2 * nchunks:2],
                                               OP.add), [t1])
        scores_free = {0: None, 1: None}
        outh_free = {0: None, 1: None}
        pt_free = None
        scs_free = {0: None, 1: None}
        atnb_free = {0: None, 1: None}
        atnt_free = {0: None, 1: None}
        oev = None

        NHEADS = 0 if stop == 21 else 16
        SKIP_AV = (stop == 22)
        ga_jobs = list(range(32, 48))
        ga_last_mm = None
        head_idx = 0
        for ti in range(8):
            sl = ti % 2
            vslot_mark[sl] = produce_v(ti + 2, vslot[sl], vslot_readers[sl])
            vslot_readers[sl] = []
            for h in range(NHEADS):
                par = h % 2
                qkmm = None
                for l in range(4):
                    qkmm = T(lambda t_, h=h, l=l, ti=ti: t_.matmul(
                        pEF[:, l * 128:(l + 1) * 128],
                        wkt[:, (h * 4 + l) * 128:(h * 4 + l + 1) * 128],
                        q2[:, 1024 * h + 128 * ti:1024 * h + 128 * (ti + 1)],
                        start=True, stop=True),
                        [m_wkt, q_mark, qk_free, pef_free[0]] if l == 0 else [], inc=(l == 3))
                qk_ev = S(lambda s: s.activation(qk_region, pEF[:, 0:512], AF.Identity), [qkmm])
                qk_free = qk_ev
                pef_free[0] = qk_ev
                ps = pAB if par == 0 else pCD
                scmm = None
                for l in range(4):
                    T(lambda t_, l=l, ti=ti, ps=ps: t_.matmul(
                        ps[:, 0:256], qk_region[:, l * 128:(l + 1) * 128],
                        kvb[:, 1168 * l + 16 + 128 * ti:1168 * l + 16 + 128 * ti + 256],
                        start=(l == 0), stop=(l == 3)),
                        [qk_ev, kv_done, scores_free[par]] if l == 0 else [])
                for l in range(4):
                    scmm = T(lambda t_, l=l, ps=ps: t_.matmul(
                        ps[:, 256:272], qk_region[:, l * 128:(l + 1) * 128],
                        kvb[:, 1168 * l:1168 * l + 16],
                        start=(l == 0), stop=(l == 3)), [], inc=(l == 3))
                mcol = 0 if ti == 0 else (272 if ti == 1 else 544)
                sc = scs[:, par * 272:(par + 1) * 272]
                madd = V(lambda v, sc=sc, ps=ps, mcol=mcol: v.tensor_tensor(
                    sc, ps[:, 0:272], masks[:, mcol:mcol + 272], OP.add),
                    [scmm, m_masks, scs_free[par]])
                scores_free[par] = madd
                mmax = V(lambda v, sc=sc, par=par: v.reduce_max(aux[:, par:par + 1], sc, axis=AX.X, negate=True),
                         [madd])
                mexp = S(lambda s, sc=sc, par=par: s.activation(sc, sc, AF.Exp, bias=aux[:, par:par + 1],
                                                                accum_out=aux[:, 2 + par:3 + par]), [mmax])
                mrcp = V(lambda v, par=par: v.reciprocal(aux[:, 4 + par:5 + par], aux[:, 2 + par:3 + par]),
                         [mexp])
                ab = smx[:, 768 + par * 272:768 + (par + 1) * 272]
                mab = V(lambda v, sc=sc, ab=ab, par=par: v.tensor_scalar(
                    ab, sc, aux[:, 4 + par:5 + par], None, OP.mult), [mrcp, atnb_free[par]])
                scs_free[par] = mab
                if SKIP_AV:
                    oev = mab
                    outh_free[par] = mab
                    atnb_free[par] = mab
                    continue
                tp = T(lambda t_, ab=ab: t_.matmul(pT[:, 0:128], ab[:, 0:128], ident[:],
                                                   start=True, stop=True, is_transpose=True),
                       [mab, m_ident, pt_free])
                tp = T(lambda t_, ab=ab: t_.matmul(pT[:, 128:256], ab[:, 128:256], ident[:],
                                                   start=True, stop=True, is_transpose=True), [])
                tp = T(lambda t_, ab=ab: t_.matmul(pT[0:16, 256:384], ab[:, 256:272], ident[:],
                                                   start=True, stop=True, is_transpose=True), [], inc=True)
                atnb_free[par] = tp
                at = smx[:, par * 384:(par + 1) * 384]
                mat = V(lambda v, at=at: v.tensor_copy(at, pT[:, 0:384]), [tp, atnt_free[par]])
                pt_free = mat
                po = pAB if par == 0 else pCD
                vprev = vslot[(ti + 1) % 2]
                vself = vslot[ti % 2]
                T(lambda t_, at=at, vprev=vprev, h=h, po=po: t_.matmul(
                    po[:, 512:640], vprev[:, h * 128:(h + 1) * 128], at[:, 0:128],
                    start=True, stop=False),
                    [mat, vslot_mark[(ti + 1) % 2], outh_free[par]])
                T(lambda t_, at=at, vself=vself, h=h, po=po: t_.matmul(
                    po[:, 512:640], vself[:, h * 128:(h + 1) * 128], at[:, 128:256],
                    start=False, stop=False), [vslot_mark[ti % 2]])
                av = T(lambda t_, at=at, h=h, po=po: t_.matmul(
                    po[:, 512:640], vper[0:16, h * 128:(h + 1) * 128], at[0:16, 256:384],
                    start=False, stop=True), [vper_mark], inc=True)
                vslot_readers[(ti + 1) % 2].append(av)
                vslot_readers[ti % 2].append(av)
                atnt_free[par] = av
                oev = S(lambda s, h=h, ti=ti, po=po: s.activation(
                    abig[:, 1024 * h + 128 * ti:1024 * h + 128 * (ti + 1)], po[:, 512:640], AF.Identity),
                    [av, q_mark, kv_done])
                outh_free[par] = oev
                head_idx += 1
                if head_idx % 8 == 0 and ga_jobs:
                    j = ga_jobs.pop(0)
                    ga_last_mm = modp_chunk(j, 2 * (j - 32))
        oa_mark = oev

        if debug >= 3 and stop not in (21, 22):
            DX(abig[:], dbg["oa"], [oa_mark])
        if stop in (2, 21, 22):
            return stop_now([oa_mark] if oa_mark else [vslot_mark[0], vslot_mark[1], vper_mark])

        def mod_late(j0, j1, extra=()):
            mm = None
            for j in range(j0, j1):
                slot = j % 2
                slab = wst[:, slot * 2048:(slot + 1) * 2048]
                mdma = DW(d_wmod[j], slab, [wst_free[slot], modp["mn"] if slot == 0 else None])
                for d in range(16):
                    mm = T(lambda t, slab=slab, d=d, j=j, j0=j0: t.matmul(
                        pG[:, (j - j0):(j - j0) + 1], slab[:, d * 128:(d + 1) * 128], siluc[:, d:d + 1],
                        start=(d == 0), stop=(d == 15)),
                        ([mdma] + list(extra)) if d == 0 else [], inc=(d == 15))
                wst_free[slot] = mm
            return mm

        m_ga_raw = modp_combine(32, 16, ga_last_mm)
        c1 = V(lambda v: v.tensor_scalar(prm[:, 160:176], prm[:, 32:48], 0.25, None, OP.max), [m_ga_raw])
        c2 = V(lambda v: v.tensor_scalar(prm[:, 176:192], prm[:, 32:48], -0.25, None, OP.min), [m_ga_raw])
        c3 = V(lambda v: v.tensor_scalar(prm[:, 192:208], prm[:, 32:48], 0.0, None, OP.is_ge), [m_ga_raw])
        c4 = V(lambda v: v.tensor_sub(prm[:, 160:176], prm[:, 160:176], prm[:, 176:192]), [c1, c2])
        c5 = V(lambda v: v.tensor_mul(prm[:, 160:176], prm[:, 160:176], prm[:, 192:208]), [c3, c4])
        m_ga = V(lambda v: v.tensor_add(prm[:, 128:144], prm[:, 160:176], prm[:, 176:192]), [c5])

        wo_psum_free = {0: scores_free[0], 1: scores_free[1]}
        x1_marks = []
        for o in range(16):
            slot = o % 2
            slab, mdma = stream_slab(d_wo[o], slot)
            ps = pAB if o % 2 == 0 else pCD
            last = None
            for gi in range(2):
                for d in range(16):
                    last = T(lambda t, slab=slab, d=d, gi=gi, ps=ps: t.matmul(
                        ps[:, gi * 512:(gi + 1) * 512], slab[:, d * 128:(d + 1) * 128],
                        abig[:, 1024 * d + gi * 512:1024 * d + gi * 512 + 512],
                        start=(d == 0), stop=(d == 15)),
                        [mdma, oa_mark, wo_psum_free[o % 2], outh_free[o % 2]] if (gi == 0 and d == 0) else [],
                        inc=(gi == 1 and d == 15))
            ev = None
            for gi in range(2):
                ev = V(lambda v, o=o, gi=gi, ps=ps: v.scalar_tensor_tensor(
                    x_sb[:, 1024 * o + gi * 512:1024 * o + gi * 512 + 512],
                    ps[:, gi * 512:(gi + 1) * 512], prm[:, 128 + o:129 + o],
                    x_sb[:, 1024 * o + gi * 512:1024 * o + gi * 512 + 512], OP.mult, OP.add),
                    [last, m_ga])
            wo_psum_free[o % 2] = ev
            wst_free[slot] = last
            x1_marks.append(ev)
        x1_mark = x1_marks[-1]

        if debug >= 4:
            DX(x_sb[:], dbg["x1"], [x1_mark])
        if stop == 3:
            return stop_now([x1_mark])

        msbcb = mod_late(48, 80, extra=[m_ga_raw, m_ga])
        m_prm_b = V(lambda v: v.tensor_copy(prm[:, 48:80], pG[:, 0:32]), [msbcb])
        m_cb1 = V(lambda v: v.tensor_scalar(prm[:, 112:128], prm[:, 64:80], 1.0, None, OP.add), [m_prm_b])

        h2_mark = ln_own(x_sb, q2, 112, 48, m_cb1, x1_mark, [wo_psum_free[0], wo_psum_free[1]])

        if debug >= 5:
            DX(q2[:], dbg["h2"], [h2_mark])
        if stop == 4:
            return stop_now([h2_mark])


        wg_slot_free = [h2_mark, h2_mark]
        wu_slot_free = [h2_mark, h2_mark]
        g_psum_free = None
        u_psum_free = None
        hm_prev = None
        down_slot_free = [None, None]
        down_rot_free = [None, None, None]
        x_final = []

        for Q in range(4):
            hmid_last = None
            for ci in range(16):
                c = 16 * Q + ci
                par = ci % 2
                gsl = kva[:, par * 2048:(par + 1) * 2048]
                usl = kva[:, 4096 + par * 2048:4096 + (par + 1) * 2048]
                mg = DW(d_gw[c], gsl, [wg_slot_free[par]])
                mu = DW(d_uw[c], usl, [wu_slot_free[par]])
                zg = V(lambda v, gsl=gsl: v.tensor_scalar(gsl, gsl, -128.0, None, OP.add), [mg])
                zu = V(lambda v, usl=usl: v.tensor_scalar(usl, usl, -128.0, None, OP.add), [mu])
                lg = None
                lu = None
                for gi in range(2):
                    for hc in range(16):
                        lg = T(lambda t, gsl=gsl, hc=hc, gi=gi: t.matmul(
                            pAB[:, gi * 512:(gi + 1) * 512], gsl[:, hc * 128:(hc + 1) * 128],
                            q2[:, 1024 * hc + gi * 512:1024 * hc + gi * 512 + 512],
                            start=(hc == 0), stop=(hc == 15)),
                            [zg, h2_mark, g_psum_free] if (gi == 0 and hc == 0) else [],
                            inc=(gi == 1 and hc == 15))
                for gi in range(2):
                    for hc in range(16):
                        lu = T(lambda t, usl=usl, hc=hc, gi=gi: t.matmul(
                            pCD[:, gi * 512:(gi + 1) * 512], usl[:, hc * 128:(hc + 1) * 128],
                            q2[:, 1024 * hc + gi * 512:1024 * hc + gi * 512 + 512],
                            start=(hc == 0), stop=(hc == 15)),
                            [zu, u_psum_free] if (gi == 0 and hc == 0) else [],
                            inc=(gi == 1 and hc == 15))
                msil = S(lambda s, c=c: s.activation(smx[:, 0:1024], pAB[:, 0:1024], AF.Silu,
                                                     scale=qs[:, c:c + 1]), [lg, hm_prev])
                hm = V(lambda v, ci=ci, c=c: v.scalar_tensor_tensor(
                    abig[:, 1024 * ci:1024 * ci + 1024], pCD[:, 0:1024], qs[:, 64 + c:65 + c],
                    smx[:, 0:1024], OP.mult, OP.mult), [lu, msil])
                hm_prev = hm
                hmid_last = hm
                g_psum_free = msil
                u_psum_free = hm
                wg_slot_free[par] = lg
                wu_slot_free[par] = lu
                if Q == 0:
                    gb_last_mm = modp_chunk(80 + ci, 2 * ci,
                                            extra=[m_prm_b, m_cb1] if ci == 0 else [])
            if Q == 0:
                m_gb_raw = modp_combine(80, 16, gb_last_mm)
                b1 = V(lambda v: v.tensor_scalar(prm[:, 160:176], prm[:, 80:96], 0.25, None, OP.max),
                       [m_gb_raw, m_ga])
                b2 = V(lambda v: v.tensor_scalar(prm[:, 176:192], prm[:, 80:96], -0.25, None, OP.min),
                       [m_gb_raw])
                b3 = V(lambda v: v.tensor_scalar(prm[:, 192:208], prm[:, 80:96], 0.0, None, OP.is_ge),
                       [m_gb_raw])
                b4 = V(lambda v: v.tensor_sub(prm[:, 160:176], prm[:, 160:176], prm[:, 176:192]), [b1, b2])
                b5 = V(lambda v: v.tensor_mul(prm[:, 160:176], prm[:, 160:176], prm[:, 192:208]), [b3, b4])
                m_gbc = V(lambda v: v.tensor_add(prm[:, 80:96], prm[:, 160:176], prm[:, 176:192]), [b5])
                m_gbds = V(lambda v: v.tensor_tensor(prm[:, 144:160], prm[:, 80:96], qs[:, 128:144], OP.mult),
                           [m_gbc, m_qs])
                if stop == 72:
                    DX(prm[:], dbg["x1"][:, 0:208], [m_gbds])
                    return stop_now([m_gbds, h2_mark])
            if stop == 71 and Q == 0:
                DX(abig[:], dbg["oa"], [hmid_last])
                return stop_now([hmid_last])
            for k in range(16):
                par = k % 2
                dsl = kva[:, par * 2048:(par + 1) * 2048]
                md = DW(d_dw[k, Q], dsl, [down_slot_free[par], wg_slot_free[par]])
                zd = V(lambda v, dsl=dsl: v.tensor_scalar(dsl, dsl, -128.0, None, OP.add), [md])
                dlast = None
                for gi in range(2):
                    rot = (k * 2 + gi) % 3
                    ps = (pEF[:, 0:512], pEF[:, 512:1024], pG[:, 0:512])[rot]
                    last = None
                    for ci in range(16):
                        last = T(lambda t, dsl=dsl, ci=ci, gi=gi, ps=ps: t.matmul(
                            ps, dsl[:, ci * 128:(ci + 1) * 128],
                            abig[:, 1024 * ci + gi * 512:1024 * ci + gi * 512 + 512],
                            start=(ci == 0), stop=(ci == 15)),
                            [zd, hmid_last, down_rot_free[rot]] if ci == 0 else [], inc=(ci == 15))
                    ev = V(lambda v, k=k, gi=gi, ps=ps: v.scalar_tensor_tensor(
                        x_sb[:, 1024 * k + gi * 512:1024 * k + gi * 512 + 512], ps,
                        prm[:, 144 + k:145 + k],
                        x_sb[:, 1024 * k + gi * 512:1024 * k + gi * 512 + 512], OP.mult, OP.add),
                        [last, m_gbds])
                    down_rot_free[rot] = ev
                    dlast = last
                    if Q == 3:
                        x_final.append(ev)
                down_slot_free[par] = dlast
            wg_slot_free = [down_slot_free[0], down_slot_free[1]]

        m_out = DX(x_sb[:].rearrange("p (c s) -> p c s", c=16),
                   d_out.rearrange("(c p) s -> p c s", p=128))
        w, fn, inc, n = p.ops["y"].pop()
        for m in x_final:
            s_, v_ = m
            if w.get(s_, 0) < v_:
                w[s_] = v_
        p.ops["y"].append((w, fn, inc, n))
        p.op("y", lambda y: y.nop(), [("dX", p.cnt["dX"])])
        return emit_block()



_BF = ml_dtypes.bfloat16


def _prep_shared(inp):
    Wq = np.asarray(inp["L0_attn_Wq"], np.float32)
    Wkv = np.asarray(inp["L0_attn_Wkv"], np.float32)
    Wk = np.asarray(inp["L0_attn_Wk"], np.float32)
    Wv = np.asarray(inp["L0_attn_Wv"], np.float32)
    Wo = np.asarray(inp["L0_attn_Wo"], np.float32)
    wmod = np.asarray(inp["L0_wmod"], np.float32)
    gw = np.asarray(inp["L0_ffn_gate_w"], np.int32)
    uw = np.asarray(inp["L0_ffn_up_w"], np.int32)
    dw = np.asarray(inp["L0_ffn_down_w"], np.int32)
    gs = np.asarray(inp["L0_ffn_gate_scale"], np.float32)
    us = np.asarray(inp["L0_ffn_up_scale"], np.float32)
    ds = np.asarray(inp["L0_ffn_down_scale"], np.float32)
    c = np.asarray(inp["L0_c"], np.float32)

    def stat4(W, KC, MC):
        return np.ascontiguousarray(W.reshape(KC, 128, MC, 128).transpose(2, 1, 0, 3))

    sh = {}
    sh["wqr"] = stat4(Wq, 16, 16)
    sh["wor"] = stat4(Wo, 16, 16)
    sh["wmodr"] = stat4(wmod, 16, 96)
    sh["wkvl"] = stat4(Wkv, 16, 4)
    sh["wkvc"] = np.ascontiguousarray(Wkv.reshape(16, 128, 4, 128))
    sh["wktr"] = np.ascontiguousarray(
        Wk.reshape(4, 128, 16, 128).transpose(3, 2, 0, 1).reshape(128, 64, 128))
    sh["wvr"] = np.ascontiguousarray(Wv.reshape(4, 128, 2048).transpose(1, 0, 2))
    sh["gwr"] = np.ascontiguousarray(gw.reshape(64, 128, 16, 128).transpose(0, 3, 2, 1))
    sh["uwr"] = np.ascontiguousarray(uw.reshape(64, 128, 16, 128).transpose(0, 3, 2, 1))
    a = dw.reshape(16, 128, 64, 128).transpose(0, 2, 3, 1)
    sh["dwr"] = np.ascontiguousarray(a.reshape(16, 4, 16, 128, 128).transpose(0, 1, 3, 2, 4))
    qsv = np.concatenate([gs.reshape(64, 128).T, us.reshape(64, 128).T,
                          ds.reshape(16, 128).T, c.reshape(16, 128).T], axis=1)
    sh["qsv"] = np.ascontiguousarray(qsv.astype(np.float32))
    sh["identw"] = np.eye(128, dtype=np.float32).astype(_BF)
    return sh


def _mask_block(q0):
    qq = np.arange(128)[:, None]
    ki = np.concatenate([q0 - 128 + np.arange(128), q0 + np.arange(128), np.arange(16)])[None, :]
    qi = q0 + qq
    valid = (ki >= 0) & (ki <= qi) & (((qi - ki) < 128) | (ki < 16))
    pers = np.arange(16)
    dup = (pers >= q0 - 128) & (pers < q0 + 128)
    valid = valid.copy()
    valid[:, 256:272] &= ~dup[None, :]
    return np.where(valid, np.float32(0.0), np.float32(NEG))


def _masks_for_core(i):
    q0 = 1024 * i
    m_a = _mask_block(q0)
    m_b = _mask_block(q0 + 128)
    m_std = _mask_block(max(q0 + 256, 256))
    return np.ascontiguousarray(np.concatenate([m_a, m_b, m_std], axis=1)).astype(_BF)


def make_in_maps(inputs):
    x = np.asarray(inputs["L0_attn_x"], np.float32)[0]
    sh = _prep_shared(inputs)
    in_maps = []
    for i in range(8):
        r0 = 1024 * i
        own = x[r0:r0 + 1024]
        halo = x[r0 - 128:r0] if i > 0 else x[0:128]
        per = x[0:16]
        m = dict(sh)
        m["xto"] = np.ascontiguousarray(own.T)
        m["xhp"] = np.ascontiguousarray(np.concatenate([per, halo], axis=0).T)
        m["maskr"] = _masks_for_core(i)
        in_maps.append(m)
    return in_maps


def kernel(**inputs):
    nc = build(debug=0)
    in_maps = make_in_maps(inputs)
    res = run_bass_kernel_spmd(nc, in_maps, core_ids=list(range(8)))
    out = np.empty((1, 8192, 2048), np.float32)
    for i in range(8):
        out[0, 1024 * i:1024 * (i + 1), :] = res.results[i]["out"].T
    return out
